# version 50
# baseline (speedup 1.0000x reference)
import numpy as np
import ml_dtypes
import concourse.bass as bass
import concourse.mybir as mybir
from concourse.bass_utils import run_bass_kernel_spmd

F32 = mybir.dt.float32
BF16 = mybir.dt.bfloat16
I8 = mybir.dt.int8
AF = mybir.ActivationFunctionType
ALU = mybir.AluOpType
AX = mybir.AxisListType

S = 2048
D = 1024
NT = S // 128
NCH = D // 128
NEG = -1.0e30

ENGINES = ("sync", "gpsimd", "tensor", "vector", "scalar")


def ts(i, n=128):
    return slice(i * n, (i + 1) * n)


def cp(e, out, in_):
    if hasattr(e, "tensor_copy"):
        return e.tensor_copy(out, in_)
    return e.copy(out, in_)


def build_nc():
    nc = bass.Bass("TRN2", target_bir_lowering=False,
                   detect_race_conditions=False)

    x_h = nc.dram_tensor("x", [S, D], F32, kind="ExternalInput").ap()
    wk_h = nc.dram_tensor("wk", [D, D], F32, kind="ExternalInput").ap()
    wq_h = nc.dram_tensor("wq", [D, D], F32, kind="ExternalInput").ap()
    wv_h = nc.dram_tensor("wv", [D, D], F32, kind="ExternalInput").ap()
    wf_h = nc.dram_tensor("wf", [D, D], I8, kind="ExternalInput").ap()
    wsc_h = nc.dram_tensor("wsc", [128, D], F32, kind="ExternalInput").ap()
    bias_h = nc.dram_tensor("biasd", [128, D], F32, kind="ExternalInput").ap()
    ident_h = nc.dram_tensor("ident", [128, 128], BF16, kind="ExternalInput").ap()
    mask_h = nc.dram_tensor("maskc", [128, 128], F32, kind="ExternalInput").ap()
    y_h = nc.dram_tensor("y", [S, D], F32, kind="ExternalOutput").ap()

    ops = {e: [] for e in ENGINES}
    cnt = {"sx": 0, "sw": 0, "sy": 0, "st": 0, "sv": 0, "ss": 0, "sg": 0}

    def op(eng, fn, waits=(), inc=None, amt=1):
        w = {}
        for s, v in waits:
            if v > 0:
                w[s] = max(w.get(s, 0), v)
        ops[eng].append((fn, sorted(w.items()), inc, amt))
        if inc is not None:
            cnt[inc] += amt
        return cnt[inc] if inc is not None else None

    def dma(eng, dst, src, waits=(), inc="sx"):
        return op(eng, lambda e, d=dst, s=src: e.dma_start(out=d, in_=s),
                  waits=waits, inc=inc, amt=16)

    import contextlib
    with contextlib.ExitStack() as ctx:
        ec = ctx.enter_context
        Kf = ec(nc.sbuf_tensor("Kf", [128, NCH, S], BF16))
        Qf = ec(nc.sbuf_tensor("Qf", [128, NCH, S], BF16))
        Vt = ec(nc.sbuf_tensor("Vt", [128, NT, D], BF16))
        watt = ec(nc.sbuf_tensor("watt", [128, 2, NCH, D], BF16))
        wqT = ec(nc.sbuf_tensor("wqT", [128, NCH, D], BF16))
        xa = ec(nc.sbuf_tensor("xa", [128, 3, D], F32))
        finb = ec(nc.sbuf_tensor("finb", [128, 2, D], F32))
        wscb = ec(nc.sbuf_tensor("wscb", [128, D], F32))
        biasb = ec(nc.sbuf_tensor("biasb", [128, D], F32))
        identb = ec(nc.sbuf_tensor("identb", [128, 128], BF16))
        maskb = ec(nc.sbuf_tensor("maskb", [128, 128], F32))
        stats = ec(nc.sbuf_tensor("stats", [128, 36], F32))
        cvals = ec(nc.sbuf_tensor("cvals", [128, 4], F32))
        arena = ec(nc.sbuf_tensor("arena", [128, 34816], mybir.dt.uint8))
        base = nc.lookup_mloc(arena).addr
        h1f = nc.alloc_sbuf_tensor_at("h1f", [128, NCH, S], BF16,
                                      offset=base + 0)
        h1t = nc.alloc_sbuf_tensor_at("h1t", [128, 1, D], BF16,
                                      offset=base + 32768)
        s32 = nc.alloc_sbuf_tensor_at("s32", [128, S], BF16, offset=base + 0)
        x2b = nc.alloc_sbuf_tensor_at("x2b", [128, 2, D], F32,
                                      offset=base + 4096)
        h2b = nc.alloc_sbuf_tensor_at("h2b", [128, D], F32,
                                      offset=base + 12288)
        ubuf = nc.alloc_sbuf_tensor_at("ubuf", [128, D], F32,
                                       offset=base + 16384)
        rbuf = nc.alloc_sbuf_tensor_at("rbuf", [128, D], F32,
                                       offset=base + 20480)
        xqb = nc.alloc_sbuf_tensor_at("xqb", [128, D], BF16,
                                      offset=base + 24576)
        xqf = nc.alloc_sbuf_tensor_at("xqf", [128, NCH, 128], BF16,
                                      offset=base + 24576)
        Abuf = nc.alloc_sbuf_tensor_at("Abuf", [128, S], BF16,
                                       offset=base + 26624)
        ATb = nc.alloc_sbuf_tensor_at("ATb", [128, NT, 128], BF16,
                                      offset=base + 30720)
        wbase = nc.lookup_mloc(watt).addr
        h2b1 = nc.alloc_sbuf_tensor_at("h2b1", [128, D], F32, offset=wbase)
        ubuf1 = nc.alloc_sbuf_tensor_at("ubuf1", [128, D], F32,
                                        offset=wbase + 4096)
        xqb1 = nc.alloc_sbuf_tensor_at("xqb1", [128, D], BF16,
                                       offset=wbase + 8192)
        xqf1 = nc.alloc_sbuf_tensor_at("xqf1", [128, NCH, 128], BF16,
                                       offset=wbase + 8192)
        x2c = nc.alloc_sbuf_tensor_at("x2c", [128, D], F32,
                                      offset=wbase + 10240)
        s32b = nc.alloc_sbuf_tensor_at("s32b", [128, S], BF16,
                                       offset=wbase + 16384)
        bbase = nc.lookup_mloc(biasb).addr
        h1t2 = nc.alloc_sbuf_tensor_at("h1t2", [128, 1, D], BF16,
                                       offset=bbase)
        pbA = ec(nc.psum_tensor("pbA", [128, 512], F32))
        pbB = ec(nc.psum_tensor("pbB", [128, 512], F32))
        pbS0 = ec(nc.psum_tensor("pbS0", [128, 512], F32))
        pbS1 = ec(nc.psum_tensor("pbS1", [128, 512], F32))
        pbT0 = ec(nc.psum_tensor("pbT0", [128, 8, 128], BF16))
        pbT1 = ec(nc.psum_tensor("pbT1", [128, 8, 128], BF16))
        pbO = ec(nc.psum_tensor("pbO", [128, D], F32))
        SX = ec(nc.semaphore("sx"))
        SW = ec(nc.semaphore("sw"))
        SY = ec(nc.semaphore("sy"))
        ST = ec(nc.semaphore("st"))
        SV = ec(nc.semaphore("sv"))
        SS = ec(nc.semaphore("ss"))
        SG = ec(nc.semaphore("sg"))
        block = ec(nc.Block())
        sems = {"sx": SX, "sw": SW, "sy": SY, "st": ST, "sv": SV,
                "ss": SS, "sg": SG}

        sl = lambda k: stats[:, k:k + 1]
        sums = sl(0)
        sumsq = sl(1)
        mvec = sl(2)
        msq = sl(3)
        var = sl(4)
        rstd = sl(5)
        stdv = sl(15)
        negmr = sl(6)
        maxs = sl(7)
        negmax = sl(8)
        sumexp = sl(9)
        recip = sl(10)
        amax = sl(11)
        amax_e = sl(12)
        sums_p = (sl(0), sl(20))
        sumsq_p = (sl(1), sl(21))
        mvec_p = (sl(2), sl(22))
        t1s_p = (sl(18), sl(23))
        varr2_p = (sl(19), sl(24))
        stdv_p = (sl(15), sl(25))
        rstd_p = (sl(5), sl(26))
        negmr_p = (sl(6), sl(27))
        amax_p = (sl(11), sl(28))
        amaxe_p = (sl(12), sl(29))
        xsinv0_p = (sl(3), sl(30))
        xsinv_p = (sl(14), sl(31))
        xsv_p3 = (sl(13), sl(16), sl(32))

        cv_eps = cvals[:, 0:1]
        cv_128h = cvals[:, 1:2]
        cv_s32 = cvals[:, 2:3]
        cv_nv = cvals[:, 3:4]

        pb_free = {}

        op("vector", lambda e: e.memset(cv_eps, 1e-5))
        op("vector", lambda e: e.memset(cv_128h, 128.5))
        op("vector", lambda e: e.memset(cv_s32, 1.0 / 32.0))
        op("vector", lambda e: e.memset(cv_nv, -1.0 / 1024.0), inc="sv")
        cv_done = cnt["sv"]
        c_ident = dma("sync", identb[:, :], ident_h[:, :])
        c_mask = dma("sync", maskb[:, :], mask_h[:, :])
        c_wsc = dma("sync", wscb[:, :], wsc_h[:, :])

        for c in range(NCH):
            sw_wk = dma("gpsimd", watt[:, 0, c, :], wk_h[ts(c), :], inc="sw")
        for c in range(NCH):
            sw_wq = dma("gpsimd", watt[:, 1, c, :], wq_h[ts(c), :], inc="sw")

        h1_cnt = [0] * NT
        trans_cnt = [0] * NT
        evac_cnt = [0] * NT
        negmr_cnt = [0] * NT
        var_cnt = [0] * NT
        sx_x = [0] * NT

        k_evac_all = 0
        q_evac_all = 0

        pa = lambda base_, tc: stats[:, base_ + 8 * (tc % 2):base_ + 8 * (tc % 2) + 1]

        rsum_cnt = [0] * NT
        sq_cnt = [0] * NT

        def plan_ln_front(tc):
            b = tc % 3
            w_x = [("ss", h1_cnt[tc - 3])] if tc >= 3 else []
            sx_x[tc] = dma("sync", xa[:, b, :], x_h[ts(tc), :], waits=w_x)
            rsum_cnt[tc] = op("vector",
                              lambda e, b=b, tc=tc: e.reduce_sum(
                                  pa(0, tc), xa[:, b, :], axis=AX.X),
                              waits=[("sx", sx_x[tc])], inc="sv")
            sq_cnt[tc] = op("vector",
                            lambda e, b=b, tc=tc: e.scalar_tensor_tensor(
                                finb[:, 0, :], xa[:, b, :], 1.0,
                                xa[:, b, :], ALU.mult, ALU.mult,
                                accum_out=pa(1, tc)),
                            waits=[("sx", sx_x[tc])], inc="sv")

        def plan_ln_back(tc):
            b = tc % 3
            b2 = tc % 2
            c2 = op("vector",
                    lambda e, tc=tc: e.tensor_scalar(
                        pa(3, tc), pa(0, tc), pa(0, tc), 1.0 / 1024.0,
                        ALU.mult, ALU.mult),
                    waits=[("sv", rsum_cnt[tc])], inc="sv")
            c3 = op("scalar",
                    lambda e, tc=tc: e.activation(pa(4, tc), pa(1, tc),
                                                  AF.Identity, bias=pa(3, tc),
                                                  scale=-1.0),
                    waits=[("sv", max(c2, sq_cnt[tc]))], inc="ss")
            std_c = op("scalar",
                       lambda e, tc=tc: e.activation(pa(5, tc), pa(4, tc),
                                                     AF.Sqrt, bias=cv_eps,
                                                     scale=cv_nv),
                       waits=[("ss", c3)], inc="ss")
            rr_c = op("vector",
                      lambda e, tc=tc: e.reciprocal(pa(6, tc), pa(5, tc)),
                      waits=[("ss", std_c)], inc="sv")
            negmr_cnt[tc] = op("vector",
                               lambda e, tc=tc: e.tensor_scalar(
                                   pa(7, tc), pa(0, tc), pa(6, tc),
                                   -1.0 / 1024.0, ALU.mult, ALU.mult),
                               waits=[("sv", rr_c)], inc="sv")
            w_h1 = [("sv", negmr_cnt[tc])]
            if tc >= 2:
                w_h1.append(("st", trans_cnt[tc - 2]))
            h1s = h1t if tc % 2 == 0 else h1t2
            h1_cnt[tc] = op("scalar",
                            lambda e, b=b, tc=tc, h1s=h1s: e.activation(
                                h1s[:, 0, :], xa[:, b, :], AF.Identity,
                                bias=pa(7, tc), scale=pa(6, tc)),
                            waits=w_h1, inc="ss")
            pbT = pbT0 if b2 == 0 else pbT1
            pn = "pbT0" if b2 == 0 else "pbT1"
            w_t = [("ss", h1_cnt[tc]), ("sx", c_ident)]
            if pn in pb_free:
                w_t.append(pb_free[pn])
            for c in range(NCH):
                last = c == NCH - 1
                trans_cnt[tc] = op(
                    "tensor",
                    lambda e, c=c, pbT=pbT, h1s=h1s: e.transpose(
                        pbT[:, c, :], h1s[:, 0, ts(c)], identb[:, :]),
                    waits=w_t if c == 0 else (),
                    inc="st" if last else None)

        def plan_h1_evac(tc):
            pbT = pbT0 if tc % 2 == 0 else pbT1
            pn = "pbT0" if tc % 2 == 0 else "pbT1"
            evac_cnt[tc] = op("vector",
                              lambda e, pbT=pbT, tc=tc: cp(
                                  e, h1f[:, :, ts(tc)], pbT[:, :, :]),
                              waits=[("st", trans_cnt[tc])], inc="sv")
            pb_free[pn] = ("sv", evac_cnt[tc])

        def plan_proj_window(w, wbuf, dstT, evac_eng, evac_inc):
            ret = 0
            pbs4 = (("pbA", pbA), ("pbB", pbB), ("pbS0", pbS0), ("pbS1", pbS1))
            for g in range(NCH):
                pn, pb = pbs4[g % 4]
                w_mm = [("sw", 128 * (wbuf + 1)), ("sv", evac_cnt[4 * w + 3])]
                if pn in pb_free:
                    w_mm.append(pb_free[pn])
                grp = 0
                for c in range(NCH):
                    grp = op("tensor",
                             lambda e, c=c, g=g, pb=pb, w=w, wbuf=wbuf: e.matmul(
                                 pb[:, :], watt[:, wbuf, c, ts(g)],
                                 h1f[:, c, ts(w, 512)],
                                 start=(c == 0), stop=(c == NCH - 1)),
                             waits=w_mm if c == 0 else (),
                             inc="st" if c == NCH - 1 else None)
                ec = op(evac_eng,
                        lambda e, g=g, pb=pb, w=w, dstT=dstT: cp(e, 
                            dstT[:, g, ts(w, 512)], pb[:, :]),
                        waits=[("st", grp)], inc=evac_inc)
                pb_free[pn] = (evac_inc, ec)
                ret = ec
            return ret

        def plan_vtm(rng):
            for tcv in rng:
                w_mm = [("sw", 384), ("sv", evac_cnt[tcv])]
                if "pbO" in pb_free:
                    w_mm.append(pb_free["pbO"])
                grp = 0
                for j in range(2):
                    for c in range(NCH):
                        last = j == 1 and c == NCH - 1
                        grp = op("tensor",
                                 lambda e, c=c, j=j, tcv=tcv: e.matmul(
                                     pbO[:, ts(j, 512)],
                                     h1f[:, c, ts(tcv)],
                                     wqT[:, c, ts(j, 512)],
                                     start=(c == 0), stop=(c == NCH - 1)),
                                 waits=w_mm if (j == 0 and c == 0) else (),
                                 inc="st" if last else None)
                vt_evac = op("scalar",
                             lambda e, tcv=tcv: cp(e, Vt[:, tcv, :],
                                                   pbO[:, :]),
                             waits=[("st", grp)], inc="ss")
                pb_free["pbO"] = ("ss", vt_evac)

        plan_ln_front(0)
        for tc in range(NT):
            plan_ln_back(tc)
            if tc >= 2:
                plan_h1_evac(tc - 2)
            if tc + 1 < NT:
                plan_ln_front(tc + 1)
            if tc % 4 == 1 and tc >= 5:
                w = (tc - 5) // 4
                k_evac_all = plan_proj_window(w, 0, Kf, "scalar", "ss")
                q_evac_all = plan_proj_window(w, 1, Qf, "vector", "sv")
            if tc == 15:
                plan_vtm(range(0, 8))
        plan_h1_evac(NT - 2)
        plan_h1_evac(NT - 1)
        k_evac_all = plan_proj_window(3, 0, Kf, "scalar", "ss")
        q_evac_all = plan_proj_window(3, 1, Qf, "vector", "sv")
        st_q_done = cnt["st"]

        c_bias = dma("sync", biasb[:, :], bias_h[:, :],
                     waits=[("st", trans_cnt[NT - 1])])

        plan_vtm(range(8, NT))

        for c in range(NCH):
            dma("gpsimd", wqT[:, c, :], wv_h[ts(c), :], waits=(), inc="sw")
        for g in range(NCH):
            dma("gpsimd", watt[:, 1, g, :], wf_h[ts(g), :],
                waits=[("st", st_q_done)] if g == 0 else (), inc="sw")

        for c in range(NCH):
            pbT = pbT0 if c % 2 == 0 else pbT1
            pn = "pbT0" if c % 2 == 0 else "pbT1"
            w_t = [("sw", 512), ("sx", c_ident)]
            if pn in pb_free:
                w_t.append(pb_free[pn])
            grp = 0
            for g in range(NCH):
                grp = op("tensor",
                         lambda e, c=c, g=g, pbT=pbT: e.transpose(
                             pbT[:, g, :], watt[:, 1, g, ts(c)], identb[:, :]),
                         waits=w_t if g == 0 else (),
                         inc="st" if g == NCH - 1 else None)
            ec = op("vector",
                    lambda e, c=c, pbT=pbT: cp(e, wqT[:, c, :],
                                                          pbT[:, :, :]),
                    waits=[("st", grp)], inc="sv")
            pb_free[pn] = ("sv", ec)

        x2stt_cnt = [0] * NT
        at_grp = [0] * NT
        av_grp = [0] * NT
        fin_cnt = [0] * NT
        y_cnt = [0] * NT
        sx_xb = [0] * NT

        sc_evac = [0] * (NT + 1)
        nm_cnt = [0] * (NT + 1)
        rm_cnt = [0] * (NT + 1)
        exp_cnt = [0] * (NT + 1)
        rc_cnt = [0] * NT
        xqT_grp = [0] * NT
        xqf_cnt = [0] * NT
        ffn_grp = [0] * NT
        fin1_cnt = [0] * NT
        fin2_cnt = [0] * NT
        u_cnt = [0] * NT

        H2B = (h2b, h2b1)
        UB = (ubuf, ubuf1)
        XQB = (xqb, xqb1)
        XQF = (xqf, xqf1)

        def X2(q):
            r = q % 3
            return x2b[:, r, :] if r < 2 else x2c[:, :]

        def S32(j):
            return s32 if j % 2 == 0 else s32b

        def plan_scores(qt):
            L = (qt + 1) * 128
            nw = (L + 511) // 512
            ev_last = 0
            for w in range(nw):
                cw = min(512, L - 512 * w)
                pb = pbS0 if w % 2 == 0 else pbS1
                pn = "pbS0" if w % 2 == 0 else "pbS1"
                w_mm = []
                if qt == 0:
                    w_mm = [("ss", k_evac_all), ("sv", q_evac_all)]
                if pn in pb_free:
                    w_mm.append(pb_free[pn])
                grp = 0
                for c in range(NCH):
                    grp = op("tensor",
                             lambda e, c=c, qt=qt, w=w, cw=cw, pb=pb: e.matmul(
                                 pb[:, 0:cw], Qf[:, c, ts(qt)],
                                 Kf[:, c, 512 * w:512 * w + cw],
                                 start=(c == 0), stop=(c == NCH - 1)),
                             waits=w_mm if c == 0 else (),
                             inc="st" if c == NCH - 1 else None)
                w_ev = [("st", grp)]
                if qt >= 2 and w == 0:
                    w_ev.append(("sv", nm_cnt[qt - 2]))
                ev_last = op("scalar",
                             lambda e, w=w, cw=cw, pb=pb, qt=qt: cp(
                                 e, S32(qt)[:, 512 * w:512 * w + cw],
                                 pb[:, 0:cw]),
                             waits=w_ev, inc="ss")
                pb_free[pn] = ("ss", ev_last)
            sc_evac[qt] = ev_last

        def plan_smprep_mask(qt):
            L = (qt + 1) * 128
            op("vector",
               lambda e, L=L, qt=qt: e.tensor_tensor(
                   S32(qt)[:, L - 128:L], S32(qt)[:, L - 128:L],
                   maskb[:, :], ALU.add),
               waits=[("ss", sc_evac[qt]), ("sx", c_mask)])

        def plan_smprep_rmax(qt):
            L = (qt + 1) * 128
            rm_cnt[qt] = op("vector",
                            lambda e, L=L, qt=qt: e.reduce_max(
                                maxs, S32(qt)[:, 0:L], axis=AX.X,
                                negate=True),
                            inc="sv")

        def plan_smprep_nm(qt):
            nm_cnt[qt] = op("vector",
                           lambda e: e.tensor_scalar(negmax, maxs, 1.0 / 32.0,
                                                     None, ALU.mult),
                           waits=[("sv", rm_cnt[qt])], inc="sv")

        def plan_smprep(qt):
            plan_smprep_mask(qt)
            plan_smprep_rmax(qt)
            plan_smprep_nm(qt)

        def plan_exp(qt):
            L = (qt + 1) * 128
            w_exp = [("sv", nm_cnt[qt])]
            if qt >= 1:
                w_exp.append(("st", at_grp[qt - 1]))
            exp_cnt[qt] = op("scalar",
                             lambda e, L=L, qt=qt: e.activation(
                                 Abuf[:, 0:L], S32(qt)[:, 0:L], AF.Exp,
                                 bias=negmax, scale=cv_s32,
                                 accum_out=sumexp),
                             waits=w_exp, inc="ss")

        def plan_ffn_branch(fq):
            par = fq % 2
            pbT = pbT0 if fq % 2 == 0 else pbT1
            pn = "pbT0" if fq % 2 == 0 else "pbT1"
            w_t = [("sv", xq_cnt[fq])]
            if pn in pb_free:
                w_t.append(pb_free[pn])
            grp = 0
            for c in range(NCH):
                grp = op("tensor",
                         lambda e, c=c, pbT=pbT, par=par: e.transpose(
                             pbT[:, c, :], XQB[par][:, ts(c)], identb[:, :]),
                         waits=w_t if c == 0 else (),
                         inc="st" if c == NCH - 1 else None)
            xqT_grp[fq] = grp
            xqf_cnt[fq] = op("scalar",
                             lambda e, pbT=pbT, par=par: cp(
                                 e, XQF[par][:, :, :], pbT[:, :, :]),
                             waits=[("st", grp)], inc="ss")
            pb_free[pn] = ("ss", xqf_cnt[fq])
            grp = 0
            for j, pbF in ((0, pbA), (1, pbB)):
                pfn = "pbA" if j == 0 else "pbB"
                w_mm = [("ss", xqf_cnt[fq])]
                if pfn in pb_free:
                    w_mm.append(pb_free[pfn])
                for c in range(NCH):
                    grp = op("tensor",
                             lambda e, j=j, c=c, pbF=pbF, par=par: e.matmul(
                                 pbF[:, :], XQF[par][:, c, :],
                                 wqT[:, c, ts(j, 512)],
                                 start=(c == 0), stop=(c == NCH - 1)),
                             waits=w_mm if c == 0 else (),
                             inc="st" if c == NCH - 1 else None)
            ffn_grp[fq] = grp

        def plan_ffn_tail(fq):
            par = fq % 2
            xsv = xsv_p3[fq % 3]
            for j, pbF in ((0, pbA), (1, pbB)):
                t1c = op("vector",
                         lambda e, j=j, pbF=pbF, xsv=xsv, par=par:
                         e.scalar_tensor_tensor(
                             finb[:, par, ts(j, 512)], pbF[:, :], xsv,
                             wscb[:, ts(j, 512)], ALU.mult, ALU.mult),
                         waits=[("st", ffn_grp[fq]), ("sx", c_wsc)],
                         inc="sv")
                pb_free["pbA" if j == 0 else "pbB"] = ("sv", t1c)
            w_fin = []
            if fq >= 2:
                w_fin.append(("sy", y_cnt[fq - 2]))
            t1_last = cnt["sv"]
            fin1_cnt[fq] = op("gpsimd",
                              lambda e, par=par, fq=fq: e.tensor_tensor(
                                  finb[:, par, :], finb[:, par, :],
                                  X2(fq), ALU.add),
                              waits=w_fin + [("sv", t1_last)], inc="sg")
            fin2_cnt[fq] = op("gpsimd",
                              lambda e, par=par: e.tensor_tensor(
                                  finb[:, par, :], finb[:, par, :],
                                  biasb[:, :], ALU.add),
                              waits=[("sx", c_bias)], inc="sg")
            y_cnt[fq] = dma("gpsimd", y_h[ts(fq), :], finb[:, par, :],
                            waits=[("sg", fin2_cnt[fq])], inc="sy")

        H0 = slice(0, 512)
        H1 = slice(512, 1024)

        def plan_cascade(q):
            par = q % 2
            ub, mk = UB[par], H2B[par]
            op("vector",
               lambda e, ub=ub, mk=mk: e.tensor_scalar(
                   mk[:, :], ub[:, :], 128.0, -128.0, ALU.is_ge, ALU.mult),
               waits=[("sv", u_cnt[q])])
            op("vector",
               lambda e, ub=ub, mk=mk: e.tensor_tensor(
                   rbuf[:, :], ub[:, :], mk[:, :], ALU.add))
            for p in (64.0, 32.0, 16.0, 8.0, 4.0, 2.0, 1.0):
                op("vector",
                   lambda e, p=p, mk=mk: e.tensor_scalar(
                       mk[:, :], rbuf[:, :], p, -p, ALU.is_ge, ALU.mult))
                op("vector",
                   lambda e, mk=mk: e.tensor_tensor(
                       rbuf[:, :], rbuf[:, :], mk[:, :], ALU.add))
            w_xq = []
            if q >= 2:
                w_xq.append(("st", xqT_grp[q - 2]))
            xq_cnt[q] = op("vector",
                           lambda e, ub=ub, par=par: e.scalar_tensor_tensor(
                               XQB[par][:, :], ub[:, :], -128.0, rbuf[:, :],
                               ALU.add, ALU.subtract),
                           waits=w_xq, inc="sv")

        xq_cnt = [0] * NT

        plan_scores(0)
        plan_scores(1)
        plan_smprep(0)
        plan_exp(0)

        for qt in range(NT):
            L = (qt + 1) * 128
            par = qt % 2

            b3 = qt % 3
            if qt >= 3:
                w_xb = [("sv", x2stt_cnt[qt - 3])]
            else:
                w_xb = [("ss", h1_cnt[[15, 13, 14][qt]])]
            sx_xb[qt] = dma("sync", xa[:, b3, :], x_h[ts(qt), :], waits=w_xb)

            nblk = qt + 1
            at_evac_last = 0
            for r in range((nblk + 7) // 8):
                k0, k1 = 8 * r, min(nblk, 8 * r + 8)
                pbT = pbT0 if r % 2 == 0 else pbT1
                pn = "pbT0" if r % 2 == 0 else "pbT1"
                w_t = [("ss", exp_cnt[qt])]
                if pn in pb_free:
                    w_t.append(pb_free[pn])
                grp = 0
                for k in range(k0, k1):
                    grp = op("tensor",
                             lambda e, k=k, k0=k0, pbT=pbT: e.transpose(
                                 pbT[:, k - k0, :], Abuf[:, ts(k)],
                                 identb[:, :]),
                             waits=w_t if k == k0 else (),
                             inc="st" if k == k1 - 1 else None)
                at_grp[qt] = grp
                at_evac_last = op(
                    "scalar",
                    lambda e, k0=k0, k1=k1, pbT=pbT: cp(
                        e, ATb[:, k0:k1, :], pbT[:, 0:k1 - k0, :]),
                    waits=[("st", grp)] +
                          ([("st", av_grp[qt - 1])] if (qt >= 1 and r == 0)
                           else []),
                    inc="ss")
                pb_free[pn] = ("ss", at_evac_last)

            w_av = [("ss", at_evac_last)]
            if "pbO" in pb_free:
                w_av.append(pb_free["pbO"])
            grp = 0
            for j in range(2):
                for k in range(nblk):
                    last = j == 1 and k == nblk - 1
                    grp = op("tensor",
                             lambda e, j=j, k=k, nb=nblk: e.matmul(
                                 pbO[:, ts(j, 512)], ATb[:, k, :],
                                 Vt[:, k, ts(j, 512)],
                                 start=(k == 0), stop=(k == nb - 1)),
                             waits=w_av if (j == 0 and k == 0) else (),
                             inc="st" if last else None)
            av_grp[qt] = grp

            rc_cnt[qt] = op("vector", lambda e: e.reciprocal(recip, sumexp),
                            waits=[("ss", exp_cnt[qt])], inc="sv")
            if qt + 1 < NT:
                plan_smprep(qt + 1)
                plan_exp(qt + 1)
            x2stt_cnt[qt] = op(
                "vector",
                lambda e, par=par, b3=b3, qt=qt: e.scalar_tensor_tensor(
                    X2(qt), pbO[:, :], recip, xa[:, b3, :],
                    ALU.mult, ALU.add, accum_out=sums_p[par]),
                waits=[("st", av_grp[qt]), ("sx", sx_xb[qt]),
                       ("sv", rc_cnt[qt])] +
                      ([("sg", fin1_cnt[qt - 3])] if qt >= 3 else []),
                inc="sv")
            pb_free["pbO"] = ("sv", x2stt_cnt[qt])

            w_sq = [("sv", x2stt_cnt[qt])]
            if qt >= 2:
                w_sq.append(("sv", xq_cnt[qt - 2]))
            sq_c = op("scalar",
                      lambda e, par=par, qt=qt: e.activation(
                          UB[par][:, :], X2(qt), AF.Square,
                          accum_out=sumsq_p[par]),
                      waits=w_sq, inc="ss")
            c1 = op("scalar",
                    lambda e, par=par: e.mul(mvec_p[par], sums_p[par],
                                             1.0 / 1024.0),
                    inc="ss")
            c2 = op("scalar",
                    lambda e, par=par: e.mul(t1s_p[par], sums_p[par],
                                             mvec_p[par]),
                    waits=[("ss", c1)], inc="ss")
            c3 = op("scalar",
                    lambda e, par=par: e.activation(
                        varr2_p[par], sumsq_p[par], AF.Identity,
                        bias=t1s_p[par], scale=-1.0),
                    waits=[("ss", c2)], inc="ss")
            std_c = op("scalar",
                       lambda e, par=par: e.activation(
                           stdv_p[par], varr2_p[par], AF.Sqrt,
                           bias=cv_eps, scale=cv_nv),
                       waits=[("ss", c3)], inc="ss")

            if qt + 2 < NT:
                plan_scores(qt + 2)

            if qt >= 2:
                plan_ffn_branch(qt - 2)

            cc = {}

            def f_rr2(par=par, std_c=std_c):
                cc["rr2"] = op("vector",
                               lambda e, par=par: e.reciprocal(
                                   rstd_p[par], stdv_p[par]),
                               waits=[("ss", std_c)], inc="sv")

            def f_negmr_h2(par=par, qt=qt):
                ng = op("vector",
                        lambda e, par=par: e.tensor_scalar(
                            negmr_p[par], mvec_p[par], rstd_p[par], -1.0,
                            ALU.mult, ALU.mult),
                        waits=[("sv", cc["rr2"])], inc="sv")
                cc["h2"] = op("vector",
                              lambda e, par=par, qt=qt: e.tensor_scalar(
                                  H2B[par][:, :], X2(qt), rstd_p[par],
                                  negmr_p[par], ALU.mult, ALU.add),
                              waits=[("sv", ng)], inc="sv")

            def f_am(par=par):
                cc["am"] = op("vector",
                              lambda e, par=par: e.reduce_max(
                                  amax_p[par], H2B[par][:, :], axis=AX.X,
                                  apply_absolute_value=True),
                              waits=[("sv", cc["h2"])], inc="sv")

            def f_scales_u(par=par, qt=qt, sq_w=None):
                ae_c = op("vector",
                          lambda e, par=par: e.tensor_scalar(
                              amaxe_p[par], amax_p[par], 1e-6, None,
                              ALU.max),
                          waits=[("sv", cc["am"])], inc="sv")
                op("vector",
                   lambda e, par=par, xsv=xsv_p3[qt % 3]: e.tensor_scalar(
                       xsv, amaxe_p[par], 1.0 / 127.0, None, ALU.mult),
                   waits=[("sv", ae_c)])
                ri_c = op("vector",
                          lambda e, par=par: e.reciprocal(xsinv0_p[par],
                                                          amaxe_p[par]),
                          inc="sv")
                xs_c = op("vector",
                          lambda e, par=par: e.tensor_scalar(
                              xsinv_p[par], xsinv0_p[par], 127.0, None,
                              ALU.mult),
                          waits=[("sv", ri_c)], inc="sv")
                u_cnt[qt] = op("vector",
                               lambda e, par=par: e.tensor_scalar(
                                   UB[par][:, :], H2B[par][:, :],
                                   xsinv_p[par], cv_128h,
                                   ALU.mult, ALU.add),
                               waits=[("sv", xs_c)] +
                                     ([("ss", sq_w)] if sq_w else []),
                               inc="sv")

            if qt >= 1:
                plan_cascade(qt - 1)
            f_rr2()
            f_negmr_h2()
            f_am()
            f_scales_u(sq_w=sq_c)

            if qt >= 2:
                plan_ffn_tail(qt - 2)

        plan_ffn_branch(NT - 2)
        plan_cascade(NT - 1)
        plan_ffn_tail(NT - 2)
        plan_ffn_branch(NT - 1)
        plan_ffn_tail(NT - 1)

        op("sync", None, waits=[("sy", y_cnt[NT - 1])])

        def emitter(name):
            def go(eng):
                for fn, waits, inc, amt in ops[name]:
                    for s, v in waits:
                        eng.wait_ge(sems[s], v)
                    if fn is None:
                        continue
                    ins = fn(eng)
                    if inc is not None:
                        ins.then_inc(sems[inc], amt)
            return go

        block.sync(emitter("sync"))
        block.gpsimd(emitter("gpsimd"))
        block.tensor(emitter("tensor"))
        block.vector(emitter("vector"))
        block.scalar(emitter("scalar"))

    return nc


_NC_CACHE = None


def kernel(L0_attn_x, L0_attn_Wk, L0_attn_Wq, L0_attn_Wv,
           L0_ffn_weight_q, L0_ffn_weight_scale, L0_ffn_bias):
    global _NC_CACHE
    x = np.ascontiguousarray(np.asarray(L0_attn_x, dtype=np.float32))
    wk = np.ascontiguousarray(np.asarray(L0_attn_Wk, dtype=np.float32))
    wq = np.ascontiguousarray(np.asarray(L0_attn_Wq, dtype=np.float32))
    wv = np.ascontiguousarray(np.asarray(L0_attn_Wv, dtype=np.float32))
    wf = np.ascontiguousarray(np.asarray(L0_ffn_weight_q, dtype=np.int8))
    wsc = np.ascontiguousarray(np.asarray(L0_ffn_weight_scale,
                                          dtype=np.float32))
    bias = np.ascontiguousarray(np.asarray(L0_ffn_bias, dtype=np.float32))

    wsc_bc = np.ascontiguousarray(np.broadcast_to(wsc[None, :], (128, D)))
    bias_bc = np.ascontiguousarray(np.broadcast_to(bias[None, :], (128, D)))
    ident = np.eye(128, dtype=ml_dtypes.bfloat16)
    mask = np.triu(np.full((128, 128), NEG, dtype=np.float32), k=1)

    if _NC_CACHE is None:
        _NC_CACHE = build_nc()
    nc = _NC_CACHE

    in_maps = []
    for b in range(8):
        in_maps.append({
            "x": np.ascontiguousarray(x[b]),
            "wk": wk, "wq": wq, "wv": wv, "wf": wf,
            "wsc": wsc_bc, "biasd": bias_bc,
            "ident": ident, "maskc": mask,
        })
    res = run_bass_kernel_spmd(nc, in_maps, core_ids=list(range(8)))
    return np.stack([np.asarray(r["y"]) for r in res.results], axis=0)


# version 51 (final)
# speedup vs baseline: 1.0175x; 1.0175x over previous
import numpy as np
import ml_dtypes
import concourse.bass as bass
import concourse.mybir as mybir
from concourse.bass_utils import run_bass_kernel_spmd

F32 = mybir.dt.float32
BF16 = mybir.dt.bfloat16
I8 = mybir.dt.int8
AF = mybir.ActivationFunctionType
ALU = mybir.AluOpType
AX = mybir.AxisListType

S = 2048
D = 1024
NT = S // 128
NCH = D // 128
NEG = -1.0e30

ENGINES = ("sync", "gpsimd", "tensor", "vector", "scalar")


def ts(i, n=128):
    return slice(i * n, (i + 1) * n)


def cp(e, out, in_):
    if hasattr(e, "tensor_copy"):
        return e.tensor_copy(out, in_)
    return e.copy(out, in_)


def build_nc():
    nc = bass.Bass("TRN2", target_bir_lowering=False,
                   detect_race_conditions=False)

    x_h = nc.dram_tensor("x", [S, D], F32, kind="ExternalInput").ap()
    wk_h = nc.dram_tensor("wk", [D, D], F32, kind="ExternalInput").ap()
    wq_h = nc.dram_tensor("wq", [D, D], F32, kind="ExternalInput").ap()
    wv_h = nc.dram_tensor("wv", [D, D], F32, kind="ExternalInput").ap()
    wf_h = nc.dram_tensor("wf", [D, D], I8, kind="ExternalInput").ap()
    wsc_h = nc.dram_tensor("wsc", [128, D], F32, kind="ExternalInput").ap()
    bias_h = nc.dram_tensor("biasd", [128, D], F32, kind="ExternalInput").ap()
    ident_h = nc.dram_tensor("ident", [128, 128], BF16, kind="ExternalInput").ap()
    mask_h = nc.dram_tensor("maskc", [128, 128], F32, kind="ExternalInput").ap()
    y_h = nc.dram_tensor("y", [S, D], F32, kind="ExternalOutput").ap()

    ops = {e: [] for e in ENGINES}
    cnt = {"sx": 0, "sw": 0, "sy": 0, "st": 0, "sv": 0, "ss": 0, "sg": 0}

    def op(eng, fn, waits=(), inc=None, amt=1):
        w = {}
        for s, v in waits:
            if v > 0:
                w[s] = max(w.get(s, 0), v)
        ops[eng].append((fn, sorted(w.items()), inc, amt))
        if inc is not None:
            cnt[inc] += amt
        return cnt[inc] if inc is not None else None

    def dma(eng, dst, src, waits=(), inc="sx"):
        return op(eng, lambda e, d=dst, s=src: e.dma_start(out=d, in_=s),
                  waits=waits, inc=inc, amt=16)

    import contextlib
    with contextlib.ExitStack() as ctx:
        ec = ctx.enter_context
        Kf = ec(nc.sbuf_tensor("Kf", [128, NCH, S], BF16))
        Qf = ec(nc.sbuf_tensor("Qf", [128, NCH, S], BF16))
        Vt = ec(nc.sbuf_tensor("Vt", [128, NT, D], BF16))
        watt = ec(nc.sbuf_tensor("watt", [128, 2, NCH, D], BF16))
        wqT = ec(nc.sbuf_tensor("wqT", [128, NCH, D], BF16))
        xa = ec(nc.sbuf_tensor("xa", [128, 3, D], F32))
        finb = ec(nc.sbuf_tensor("finb", [128, 2, D], F32))
        wscb = ec(nc.sbuf_tensor("wscb", [128, D], F32))
        biasb = ec(nc.sbuf_tensor("biasb", [128, D], F32))
        identb = ec(nc.sbuf_tensor("identb", [128, 128], BF16))
        maskb = ec(nc.sbuf_tensor("maskb", [128, 128], F32))
        stats = ec(nc.sbuf_tensor("stats", [128, 48], F32))
        cvals = ec(nc.sbuf_tensor("cvals", [128, 4], F32))
        arena = ec(nc.sbuf_tensor("arena", [128, 34816], mybir.dt.uint8))
        base = nc.lookup_mloc(arena).addr
        h1f = nc.alloc_sbuf_tensor_at("h1f", [128, NCH, S], BF16,
                                      offset=base + 0)
        h1t = nc.alloc_sbuf_tensor_at("h1t", [128, 1, D], BF16,
                                      offset=base + 32768)
        s32 = nc.alloc_sbuf_tensor_at("s32", [128, S], BF16, offset=base + 0)
        x2b = nc.alloc_sbuf_tensor_at("x2b", [128, 2, D], F32,
                                      offset=base + 4096)
        h2b = nc.alloc_sbuf_tensor_at("h2b", [128, D], F32,
                                      offset=base + 12288)
        ubuf = nc.alloc_sbuf_tensor_at("ubuf", [128, D], F32,
                                       offset=base + 16384)
        rbuf = nc.alloc_sbuf_tensor_at("rbuf", [128, D], F32,
                                       offset=base + 20480)
        xqb = nc.alloc_sbuf_tensor_at("xqb", [128, D], BF16,
                                      offset=base + 24576)
        xqf = nc.alloc_sbuf_tensor_at("xqf", [128, NCH, 128], BF16,
                                      offset=base + 24576)
        Abuf = nc.alloc_sbuf_tensor_at("Abuf", [128, S], BF16,
                                       offset=base + 26624)
        ATb = nc.alloc_sbuf_tensor_at("ATb", [128, NT, 128], BF16,
                                      offset=base + 30720)
        wbase = nc.lookup_mloc(watt).addr
        h2b1 = nc.alloc_sbuf_tensor_at("h2b1", [128, D], F32, offset=wbase)
        ubuf1 = nc.alloc_sbuf_tensor_at("ubuf1", [128, D], F32,
                                        offset=wbase + 4096)
        xqb1 = nc.alloc_sbuf_tensor_at("xqb1", [128, D], BF16,
                                       offset=wbase + 8192)
        xqf1 = nc.alloc_sbuf_tensor_at("xqf1", [128, NCH, 128], BF16,
                                       offset=wbase + 8192)
        x2c = nc.alloc_sbuf_tensor_at("x2c", [128, D], F32,
                                      offset=wbase + 10240)
        s32b = nc.alloc_sbuf_tensor_at("s32b", [128, S], BF16,
                                       offset=wbase + 16384)
        bbase = nc.lookup_mloc(biasb).addr
        h1t2 = nc.alloc_sbuf_tensor_at("h1t2", [128, 1, D], BF16,
                                       offset=bbase)
        wscbase = nc.lookup_mloc(wscb).addr
        xa4 = nc.alloc_sbuf_tensor_at("xa4", [128, 1, D], F32,
                                      offset=wscbase)

        def XA(i):
            r = i % 4
            return xa[:, r, :] if r < 3 else xa4[:, 0, :]
        pbA = ec(nc.psum_tensor("pbA", [128, 512], F32))
        pbB = ec(nc.psum_tensor("pbB", [128, 512], F32))
        pbS0 = ec(nc.psum_tensor("pbS0", [128, 512], F32))
        pbS1 = ec(nc.psum_tensor("pbS1", [128, 512], F32))
        pbT0 = ec(nc.psum_tensor("pbT0", [128, 8, 128], BF16))
        pbT1 = ec(nc.psum_tensor("pbT1", [128, 8, 128], BF16))
        pbO = ec(nc.psum_tensor("pbO", [128, D], F32))
        SX = ec(nc.semaphore("sx"))
        SW = ec(nc.semaphore("sw"))
        SY = ec(nc.semaphore("sy"))
        ST = ec(nc.semaphore("st"))
        SV = ec(nc.semaphore("sv"))
        SS = ec(nc.semaphore("ss"))
        SG = ec(nc.semaphore("sg"))
        block = ec(nc.Block())
        sems = {"sx": SX, "sw": SW, "sy": SY, "st": ST, "sv": SV,
                "ss": SS, "sg": SG}

        sl = lambda k: stats[:, k:k + 1]
        sums = sl(0)
        sumsq = sl(1)
        mvec = sl(2)
        msq = sl(3)
        var = sl(4)
        rstd = sl(5)
        stdv = sl(15)
        negmr = sl(6)
        maxs = sl(7)
        negmax = sl(8)
        sumexp = sl(9)
        recip = sl(10)
        amax = sl(11)
        amax_e = sl(12)
        sums_p = (sl(0), sl(35))
        sumsq_p = (sl(1), sl(36))
        mvec_p = (sl(2), sl(37))
        t1s_p = (sl(18), sl(38))
        varr2_p = (sl(19), sl(39))
        stdv_p = (sl(15), sl(40))
        rstd_p = (sl(5), sl(41))
        negmr_p = (sl(6), sl(42))
        amax_p = (sl(11), sl(43))
        amaxe_p = (sl(12), sl(44))
        xsinv0_p = (sl(3), sl(45))
        xsinv_p = (sl(14), sl(46))
        xsv_p3 = (sl(13), sl(16), sl(47))

        cv_eps = cvals[:, 0:1]
        cv_128h = cvals[:, 1:2]
        cv_s32 = cvals[:, 2:3]
        cv_nv = cvals[:, 3:4]

        pb_free = {}

        op("vector", lambda e: e.memset(cv_eps, 1e-5))
        op("vector", lambda e: e.memset(cv_128h, 128.5))
        op("vector", lambda e: e.memset(cv_s32, 1.0 / 32.0))
        op("vector", lambda e: e.memset(cv_nv, -1.0 / 1024.0), inc="sv")
        cv_done = cnt["sv"]
        c_ident = dma("sync", identb[:, :], ident_h[:, :])
        c_mask = dma("sync", maskb[:, :], mask_h[:, :])

        for c in range(NCH):
            sw_wk = dma("gpsimd", watt[:, 0, c, :], wk_h[ts(c), :], inc="sw")
        for c in range(NCH):
            sw_wq = dma("gpsimd", watt[:, 1, c, :], wq_h[ts(c), :], inc="sw")

        h1_cnt = [0] * NT
        trans_cnt = [0] * NT
        evac_cnt = [0] * NT
        negmr_cnt = [0] * NT
        var_cnt = [0] * NT
        sx_x = [0] * NT

        k_evac_all = 0
        q_evac_all = 0

        pa = lambda base_, tc: stats[:, base_ + 8 * (tc % 4):base_ + 8 * (tc % 4) + 1]

        rsum_cnt = [0] * NT
        sq_cnt = [0] * NT

        def plan_ln_front(tc):
            w_x = [("ss", h1_cnt[tc - 4])] if tc >= 4 else []
            sx_x[tc] = dma("sync", XA(tc), x_h[ts(tc), :], waits=w_x)
            rsum_cnt[tc] = op("vector",
                              lambda e, tc=tc: e.reduce_sum(
                                  pa(0, tc), XA(tc), axis=AX.X),
                              waits=[("sx", sx_x[tc])], inc="sv")
            sq_cnt[tc] = op("vector",
                            lambda e, tc=tc: e.scalar_tensor_tensor(
                                finb[:, 0, :], XA(tc), 1.0,
                                XA(tc), ALU.mult, ALU.mult,
                                accum_out=pa(1, tc)),
                            waits=[("sx", sx_x[tc])], inc="sv")

        def plan_ln_back(tc):
            b = tc % 3
            b2 = tc % 2
            c1 = op("scalar",
                    lambda e, tc=tc: e.mul(pa(2, tc), pa(0, tc), 1.0 / 1024.0),
                    waits=[("sv", rsum_cnt[tc])], inc="ss")
            c2 = op("scalar",
                    lambda e, tc=tc: e.mul(pa(3, tc), pa(0, tc), pa(2, tc)),
                    waits=[("ss", c1)], inc="ss")
            c3 = op("scalar",
                    lambda e, tc=tc: e.activation(pa(4, tc), pa(1, tc),
                                                  AF.Identity, bias=pa(3, tc),
                                                  scale=-1.0),
                    waits=[("ss", c2), ("sv", sq_cnt[tc])], inc="ss")
            std_c = op("scalar",
                       lambda e, tc=tc: e.activation(pa(5, tc), pa(4, tc),
                                                     AF.Sqrt, bias=cv_eps,
                                                     scale=cv_nv),
                       waits=[("ss", c3)], inc="ss")
            rr_c = op("vector",
                      lambda e, tc=tc: e.reciprocal(pa(6, tc), pa(5, tc)),
                      waits=[("ss", std_c)], inc="sv")
            negmr_cnt[tc] = op("vector",
                               lambda e, tc=tc: e.tensor_scalar(
                                   pa(7, tc), pa(2, tc), pa(6, tc), -1.0,
                                   ALU.mult, ALU.mult),
                               waits=[("sv", rr_c)], inc="sv")
            w_h1 = [("sv", negmr_cnt[tc])]
            if tc >= 2:
                w_h1.append(("st", trans_cnt[tc - 2]))
            h1s = h1t if tc % 2 == 0 else h1t2
            h1_cnt[tc] = op("scalar",
                            lambda e, tc=tc, h1s=h1s: e.activation(
                                h1s[:, 0, :], XA(tc), AF.Identity,
                                bias=pa(7, tc), scale=pa(6, tc)),
                            waits=w_h1, inc="ss")
            pbT = pbT0 if b2 == 0 else pbT1
            pn = "pbT0" if b2 == 0 else "pbT1"
            w_t = [("ss", h1_cnt[tc]), ("sx", c_ident)]
            if pn in pb_free:
                w_t.append(pb_free[pn])
            for c in range(NCH):
                last = c == NCH - 1
                trans_cnt[tc] = op(
                    "tensor",
                    lambda e, c=c, pbT=pbT, h1s=h1s: e.transpose(
                        pbT[:, c, :], h1s[:, 0, ts(c)], identb[:, :]),
                    waits=w_t if c == 0 else (),
                    inc="st" if last else None)

        def plan_h1_evac(tc):
            pbT = pbT0 if tc % 2 == 0 else pbT1
            pn = "pbT0" if tc % 2 == 0 else "pbT1"
            evac_cnt[tc] = op("vector",
                              lambda e, pbT=pbT, tc=tc: cp(
                                  e, h1f[:, :, ts(tc)], pbT[:, :, :]),
                              waits=[("st", trans_cnt[tc])], inc="sv")
            pb_free[pn] = ("sv", evac_cnt[tc])

        def plan_proj_window(w, wbuf, dstT, evac_eng, evac_inc):
            ret = 0
            pbs4 = (("pbA", pbA), ("pbB", pbB), ("pbS0", pbS0), ("pbS1", pbS1))
            for g in range(NCH):
                pn, pb = pbs4[g % 4]
                w_mm = [("sw", 128 * (wbuf + 1)), ("sv", evac_cnt[4 * w + 3])]
                if pn in pb_free:
                    w_mm.append(pb_free[pn])
                grp = 0
                for c in range(NCH):
                    grp = op("tensor",
                             lambda e, c=c, g=g, pb=pb, w=w, wbuf=wbuf: e.matmul(
                                 pb[:, :], watt[:, wbuf, c, ts(g)],
                                 h1f[:, c, ts(w, 512)],
                                 start=(c == 0), stop=(c == NCH - 1)),
                             waits=w_mm if c == 0 else (),
                             inc="st" if c == NCH - 1 else None)
                ec = op(evac_eng,
                        lambda e, g=g, pb=pb, w=w, dstT=dstT: cp(e, 
                            dstT[:, g, ts(w, 512)], pb[:, :]),
                        waits=[("st", grp)], inc=evac_inc)
                pb_free[pn] = (evac_inc, ec)
                ret = ec
            return ret

        def plan_vtm(rng):
            for tcv in rng:
                w_mm = [("sw", 384), ("sv", evac_cnt[tcv])]
                if "pbO" in pb_free:
                    w_mm.append(pb_free["pbO"])
                grp = 0
                for j in range(2):
                    for c in range(NCH):
                        last = j == 1 and c == NCH - 1
                        grp = op("tensor",
                                 lambda e, c=c, j=j, tcv=tcv: e.matmul(
                                     pbO[:, ts(j, 512)],
                                     h1f[:, c, ts(tcv)],
                                     wqT[:, c, ts(j, 512)],
                                     start=(c == 0), stop=(c == NCH - 1)),
                                 waits=w_mm if (j == 0 and c == 0) else (),
                                 inc="st" if last else None)
                vt_evac = op("scalar",
                             lambda e, tcv=tcv: cp(e, Vt[:, tcv, :],
                                                   pbO[:, :]),
                             waits=[("st", grp)], inc="ss")
                pb_free["pbO"] = ("ss", vt_evac)

        plan_ln_front(0)
        for tc in range(NT):
            plan_ln_back(tc)
            if tc >= 2:
                plan_h1_evac(tc - 2)
            if tc + 1 < NT:
                plan_ln_front(tc + 1)
            if tc % 4 == 1 and tc >= 5:
                w = (tc - 5) // 4
                k_evac_all = plan_proj_window(w, 0, Kf, "scalar", "ss")
                q_evac_all = plan_proj_window(w, 1, Qf, "vector", "sv")
            if tc == 15:
                plan_vtm(range(0, 8))
        plan_h1_evac(NT - 2)
        plan_h1_evac(NT - 1)
        k_evac_all = plan_proj_window(3, 0, Kf, "scalar", "ss")
        q_evac_all = plan_proj_window(3, 1, Qf, "vector", "sv")
        st_q_done = cnt["st"]

        c_bias = dma("sync", biasb[:, :], bias_h[:, :],
                     waits=[("st", trans_cnt[NT - 1])])
        c_wsc = dma("sync", wscb[:, :], wsc_h[:, :],
                    waits=[("ss", h1_cnt[NT - 1])])

        plan_vtm(range(8, NT))

        for c in range(NCH):
            dma("gpsimd", wqT[:, c, :], wv_h[ts(c), :], waits=(), inc="sw")
        for g in range(NCH):
            dma("gpsimd", watt[:, 1, g, :], wf_h[ts(g), :],
                waits=[("st", st_q_done)] if g == 0 else (), inc="sw")

        for c in range(NCH):
            pbT = pbT0 if c % 2 == 0 else pbT1
            pn = "pbT0" if c % 2 == 0 else "pbT1"
            w_t = [("sw", 512), ("sx", c_ident)]
            if pn in pb_free:
                w_t.append(pb_free[pn])
            grp = 0
            for g in range(NCH):
                grp = op("tensor",
                         lambda e, c=c, g=g, pbT=pbT: e.transpose(
                             pbT[:, g, :], watt[:, 1, g, ts(c)], identb[:, :]),
                         waits=w_t if g == 0 else (),
                         inc="st" if g == NCH - 1 else None)
            ec = op("vector",
                    lambda e, c=c, pbT=pbT: cp(e, wqT[:, c, :],
                                                          pbT[:, :, :]),
                    waits=[("st", grp)], inc="sv")
            pb_free[pn] = ("sv", ec)

        x2stt_cnt = [0] * NT
        at_grp = [0] * NT
        av_grp = [0] * NT
        fin_cnt = [0] * NT
        y_cnt = [0] * NT
        sx_xb = [0] * NT

        sc_evac = [0] * (NT + 1)
        nm_cnt = [0] * (NT + 1)
        rm_cnt = [0] * (NT + 1)
        exp_cnt = [0] * (NT + 1)
        rc_cnt = [0] * NT
        xqT_grp = [0] * NT
        xqf_cnt = [0] * NT
        ffn_grp = [0] * NT
        fin1_cnt = [0] * NT
        fin2_cnt = [0] * NT
        u_cnt = [0] * NT

        H2B = (h2b, h2b1)
        UB = (ubuf, ubuf1)
        XQB = (xqb, xqb1)
        XQF = (xqf, xqf1)

        def X2(q):
            r = q % 3
            return x2b[:, r, :] if r < 2 else x2c[:, :]

        def S32(j):
            return s32 if j % 2 == 0 else s32b

        def plan_scores(qt):
            L = (qt + 1) * 128
            nw = (L + 511) // 512
            ev_last = 0
            for w in range(nw):
                cw = min(512, L - 512 * w)
                pb = pbS0 if w % 2 == 0 else pbS1
                pn = "pbS0" if w % 2 == 0 else "pbS1"
                w_mm = []
                if qt == 0:
                    w_mm = [("ss", k_evac_all), ("sv", q_evac_all)]
                if pn in pb_free:
                    w_mm.append(pb_free[pn])
                grp = 0
                for c in range(NCH):
                    grp = op("tensor",
                             lambda e, c=c, qt=qt, w=w, cw=cw, pb=pb: e.matmul(
                                 pb[:, 0:cw], Qf[:, c, ts(qt)],
                                 Kf[:, c, 512 * w:512 * w + cw],
                                 start=(c == 0), stop=(c == NCH - 1)),
                             waits=w_mm if c == 0 else (),
                             inc="st" if c == NCH - 1 else None)
                w_ev = [("st", grp)]
                if qt >= 2 and w == 0:
                    w_ev.append(("sv", nm_cnt[qt - 2]))
                ev_last = op("scalar",
                             lambda e, w=w, cw=cw, pb=pb, qt=qt: cp(
                                 e, S32(qt)[:, 512 * w:512 * w + cw],
                                 pb[:, 0:cw]),
                             waits=w_ev, inc="ss")
                pb_free[pn] = ("ss", ev_last)
            sc_evac[qt] = ev_last

        def plan_smprep_mask(qt):
            L = (qt + 1) * 128
            op("vector",
               lambda e, L=L, qt=qt: e.tensor_tensor(
                   S32(qt)[:, L - 128:L], S32(qt)[:, L - 128:L],
                   maskb[:, :], ALU.add),
               waits=[("ss", sc_evac[qt]), ("sx", c_mask)])

        def plan_smprep_rmax(qt):
            L = (qt + 1) * 128
            rm_cnt[qt] = op("vector",
                            lambda e, L=L, qt=qt: e.reduce_max(
                                maxs, S32(qt)[:, 0:L], axis=AX.X,
                                negate=True),
                            inc="sv")

        def plan_smprep_nm(qt):
            nm_cnt[qt] = op("vector",
                           lambda e: e.tensor_scalar(negmax, maxs, 1.0 / 32.0,
                                                     None, ALU.mult),
                           waits=[("sv", rm_cnt[qt])], inc="sv")

        def plan_smprep(qt):
            plan_smprep_mask(qt)
            plan_smprep_rmax(qt)
            plan_smprep_nm(qt)

        def plan_exp(qt):
            L = (qt + 1) * 128
            w_exp = [("sv", nm_cnt[qt])]
            if qt >= 1:
                w_exp.append(("st", at_grp[qt - 1]))
            exp_cnt[qt] = op("scalar",
                             lambda e, L=L, qt=qt: e.activation(
                                 Abuf[:, 0:L], S32(qt)[:, 0:L], AF.Exp,
                                 bias=negmax, scale=cv_s32,
                                 accum_out=sumexp),
                             waits=w_exp, inc="ss")

        def plan_ffn_branch(fq):
            par = fq % 2
            pbT = pbT0 if fq % 2 == 0 else pbT1
            pn = "pbT0" if fq % 2 == 0 else "pbT1"
            w_t = [("sv", xq_cnt[fq])]
            if pn in pb_free:
                w_t.append(pb_free[pn])
            grp = 0
            for c in range(NCH):
                grp = op("tensor",
                         lambda e, c=c, pbT=pbT, par=par: e.transpose(
                             pbT[:, c, :], XQB[par][:, ts(c)], identb[:, :]),
                         waits=w_t if c == 0 else (),
                         inc="st" if c == NCH - 1 else None)
            xqT_grp[fq] = grp
            xqf_cnt[fq] = op("scalar",
                             lambda e, pbT=pbT, par=par: cp(
                                 e, XQF[par][:, :, :], pbT[:, :, :]),
                             waits=[("st", grp)], inc="ss")
            pb_free[pn] = ("ss", xqf_cnt[fq])
            grp = 0
            for j, pbF in ((0, pbA), (1, pbB)):
                pfn = "pbA" if j == 0 else "pbB"
                w_mm = [("ss", xqf_cnt[fq])]
                if pfn in pb_free:
                    w_mm.append(pb_free[pfn])
                for c in range(NCH):
                    grp = op("tensor",
                             lambda e, j=j, c=c, pbF=pbF, par=par: e.matmul(
                                 pbF[:, :], XQF[par][:, c, :],
                                 wqT[:, c, ts(j, 512)],
                                 start=(c == 0), stop=(c == NCH - 1)),
                             waits=w_mm if c == 0 else (),
                             inc="st" if c == NCH - 1 else None)
            ffn_grp[fq] = grp

        def plan_ffn_tail(fq):
            par = fq % 2
            xsv = xsv_p3[fq % 3]
            for j, pbF in ((0, pbA), (1, pbB)):
                t1c = op("vector",
                         lambda e, j=j, pbF=pbF, xsv=xsv, par=par:
                         e.scalar_tensor_tensor(
                             finb[:, par, ts(j, 512)], pbF[:, :], xsv,
                             wscb[:, ts(j, 512)], ALU.mult, ALU.mult),
                         waits=[("st", ffn_grp[fq]), ("sx", c_wsc)],
                         inc="sv")
                pb_free["pbA" if j == 0 else "pbB"] = ("sv", t1c)
            w_fin = []
            if fq >= 2:
                w_fin.append(("sy", y_cnt[fq - 2]))
            t1_last = cnt["sv"]
            fin1_cnt[fq] = op("gpsimd",
                              lambda e, par=par, fq=fq: e.tensor_tensor(
                                  finb[:, par, :], finb[:, par, :],
                                  X2(fq), ALU.add),
                              waits=w_fin + [("sv", t1_last)], inc="sg")
            fin2_cnt[fq] = op("gpsimd",
                              lambda e, par=par: e.tensor_tensor(
                                  finb[:, par, :], finb[:, par, :],
                                  biasb[:, :], ALU.add),
                              waits=[("sx", c_bias)], inc="sg")
            y_cnt[fq] = dma("gpsimd", y_h[ts(fq), :], finb[:, par, :],
                            waits=[("sg", fin2_cnt[fq])], inc="sy")

        H0 = slice(0, 512)
        H1 = slice(512, 1024)

        def plan_cascade(q):
            par = q % 2
            ub, mk = UB[par], H2B[par]
            op("vector",
               lambda e, ub=ub, mk=mk: e.tensor_scalar(
                   mk[:, :], ub[:, :], 128.0, -128.0, ALU.is_ge, ALU.mult),
               waits=[("sv", u_cnt[q])])
            op("vector",
               lambda e, ub=ub, mk=mk: e.tensor_tensor(
                   rbuf[:, :], ub[:, :], mk[:, :], ALU.add))
            for p in (64.0, 32.0, 16.0, 8.0, 4.0, 2.0, 1.0):
                op("vector",
                   lambda e, p=p, mk=mk: e.tensor_scalar(
                       mk[:, :], rbuf[:, :], p, -p, ALU.is_ge, ALU.mult))
                op("vector",
                   lambda e, mk=mk: e.tensor_tensor(
                       rbuf[:, :], rbuf[:, :], mk[:, :], ALU.add))
            w_xq = []
            if q >= 2:
                w_xq.append(("st", xqT_grp[q - 2]))
            xq_cnt[q] = op("vector",
                           lambda e, ub=ub, par=par: e.scalar_tensor_tensor(
                               XQB[par][:, :], ub[:, :], -128.0, rbuf[:, :],
                               ALU.add, ALU.subtract),
                           waits=w_xq, inc="sv")

        xq_cnt = [0] * NT

        plan_scores(0)
        plan_scores(1)
        plan_smprep(0)
        plan_exp(0)

        for qt in range(NT):
            L = (qt + 1) * 128
            par = qt % 2

            b3 = qt % 3
            if qt >= 3:
                w_xb = [("sv", x2stt_cnt[qt - 3])]
            else:
                w_xb = [("ss", h1_cnt[12 + qt])]
            sx_xb[qt] = dma("sync", xa[:, b3, :], x_h[ts(qt), :], waits=w_xb)

            nblk = qt + 1
            at_evac_last = 0
            for r in range((nblk + 7) // 8):
                k0, k1 = 8 * r, min(nblk, 8 * r + 8)
                pbT = pbT0 if r % 2 == 0 else pbT1
                pn = "pbT0" if r % 2 == 0 else "pbT1"
                w_t = [("ss", exp_cnt[qt])]
                if pn in pb_free:
                    w_t.append(pb_free[pn])
                grp = 0
                for k in range(k0, k1):
                    grp = op("tensor",
                             lambda e, k=k, k0=k0, pbT=pbT: e.transpose(
                                 pbT[:, k - k0, :], Abuf[:, ts(k)],
                                 identb[:, :]),
                             waits=w_t if k == k0 else (),
                             inc="st" if k == k1 - 1 else None)
                at_grp[qt] = grp
                at_evac_last = op(
                    "scalar",
                    lambda e, k0=k0, k1=k1, pbT=pbT: cp(
                        e, ATb[:, k0:k1, :], pbT[:, 0:k1 - k0, :]),
                    waits=[("st", grp)] +
                          ([("st", av_grp[qt - 1])] if (qt >= 1 and r == 0)
                           else []),
                    inc="ss")
                pb_free[pn] = ("ss", at_evac_last)

            w_av = [("ss", at_evac_last)]
            if "pbO" in pb_free:
                w_av.append(pb_free["pbO"])
            grp = 0
            for j in range(2):
                for k in range(nblk):
                    last = j == 1 and k == nblk - 1
                    grp = op("tensor",
                             lambda e, j=j, k=k, nb=nblk: e.matmul(
                                 pbO[:, ts(j, 512)], ATb[:, k, :],
                                 Vt[:, k, ts(j, 512)],
                                 start=(k == 0), stop=(k == nb - 1)),
                             waits=w_av if (j == 0 and k == 0) else (),
                             inc="st" if last else None)
            av_grp[qt] = grp

            rc_cnt[qt] = op("vector", lambda e: e.reciprocal(recip, sumexp),
                            waits=[("ss", exp_cnt[qt])], inc="sv")
            if qt + 1 < NT:
                plan_smprep(qt + 1)
                plan_exp(qt + 1)
            x2stt_cnt[qt] = op(
                "vector",
                lambda e, par=par, b3=b3, qt=qt: e.scalar_tensor_tensor(
                    X2(qt), pbO[:, :], recip, xa[:, b3, :],
                    ALU.mult, ALU.add, accum_out=sums_p[par]),
                waits=[("st", av_grp[qt]), ("sx", sx_xb[qt]),
                       ("sv", rc_cnt[qt])] +
                      ([("sg", fin1_cnt[qt - 3])] if qt >= 3 else []),
                inc="sv")
            pb_free["pbO"] = ("sv", x2stt_cnt[qt])

            w_sq = [("sv", x2stt_cnt[qt])]
            if qt >= 2:
                w_sq.append(("sv", xq_cnt[qt - 2]))
            sq_c = op("scalar",
                      lambda e, par=par, qt=qt: e.activation(
                          UB[par][:, :], X2(qt), AF.Square,
                          accum_out=sumsq_p[par]),
                      waits=w_sq, inc="ss")
            c1 = op("scalar",
                    lambda e, par=par: e.mul(mvec_p[par], sums_p[par],
                                             1.0 / 1024.0),
                    inc="ss")
            c2 = op("scalar",
                    lambda e, par=par: e.mul(t1s_p[par], sums_p[par],
                                             mvec_p[par]),
                    waits=[("ss", c1)], inc="ss")
            c3 = op("scalar",
                    lambda e, par=par: e.activation(
                        varr2_p[par], sumsq_p[par], AF.Identity,
                        bias=t1s_p[par], scale=-1.0),
                    waits=[("ss", c2)], inc="ss")
            std_c = op("scalar",
                       lambda e, par=par: e.activation(
                           stdv_p[par], varr2_p[par], AF.Sqrt,
                           bias=cv_eps, scale=cv_nv),
                       waits=[("ss", c3)], inc="ss")

            if qt + 2 < NT:
                plan_scores(qt + 2)

            if qt >= 2:
                plan_ffn_branch(qt - 2)

            cc = {}

            def f_rr2(par=par, std_c=std_c):
                cc["rr2"] = op("vector",
                               lambda e, par=par: e.reciprocal(
                                   rstd_p[par], stdv_p[par]),
                               waits=[("ss", std_c)], inc="sv")

            def f_negmr_h2(par=par, qt=qt):
                ng = op("vector",
                        lambda e, par=par: e.tensor_scalar(
                            negmr_p[par], mvec_p[par], rstd_p[par], -1.0,
                            ALU.mult, ALU.mult),
                        waits=[("sv", cc["rr2"])], inc="sv")
                cc["h2"] = op("vector",
                              lambda e, par=par, qt=qt: e.tensor_scalar(
                                  H2B[par][:, :], X2(qt), rstd_p[par],
                                  negmr_p[par], ALU.mult, ALU.add),
                              waits=[("sv", ng)], inc="sv")

            def f_am(par=par):
                cc["am"] = op("vector",
                              lambda e, par=par: e.reduce_max(
                                  amax_p[par], H2B[par][:, :], axis=AX.X,
                                  apply_absolute_value=True),
                              waits=[("sv", cc["h2"])], inc="sv")

            def f_scales_u(par=par, qt=qt, sq_w=None):
                ae_c = op("vector",
                          lambda e, par=par: e.tensor_scalar(
                              amaxe_p[par], amax_p[par], 1e-6, None,
                              ALU.max),
                          waits=[("sv", cc["am"])], inc="sv")
                op("vector",
                   lambda e, par=par, xsv=xsv_p3[qt % 3]: e.tensor_scalar(
                       xsv, amaxe_p[par], 1.0 / 127.0, None, ALU.mult),
                   waits=[("sv", ae_c)])
                ri_c = op("vector",
                          lambda e, par=par: e.reciprocal(xsinv0_p[par],
                                                          amaxe_p[par]),
                          inc="sv")
                xs_c = op("vector",
                          lambda e, par=par: e.tensor_scalar(
                              xsinv_p[par], xsinv0_p[par], 127.0, None,
                              ALU.mult),
                          waits=[("sv", ri_c)], inc="sv")
                u_cnt[qt] = op("vector",
                               lambda e, par=par: e.tensor_scalar(
                                   UB[par][:, :], H2B[par][:, :],
                                   xsinv_p[par], cv_128h,
                                   ALU.mult, ALU.add),
                               waits=[("sv", xs_c)] +
                                     ([("ss", sq_w)] if sq_w else []),
                               inc="sv")

            if qt >= 1:
                plan_cascade(qt - 1)
            f_rr2()
            f_negmr_h2()
            f_am()
            f_scales_u(sq_w=sq_c)

            if qt >= 2:
                plan_ffn_tail(qt - 2)

        plan_ffn_branch(NT - 2)
        plan_cascade(NT - 1)
        plan_ffn_tail(NT - 2)
        plan_ffn_branch(NT - 1)
        plan_ffn_tail(NT - 1)

        op("sync", None, waits=[("sy", y_cnt[NT - 1])])

        def emitter(name):
            def go(eng):
                for fn, waits, inc, amt in ops[name]:
                    for s, v in waits:
                        eng.wait_ge(sems[s], v)
                    if fn is None:
                        continue
                    ins = fn(eng)
                    if inc is not None:
                        ins.then_inc(sems[inc], amt)
            return go

        block.sync(emitter("sync"))
        block.gpsimd(emitter("gpsimd"))
        block.tensor(emitter("tensor"))
        block.vector(emitter("vector"))
        block.scalar(emitter("scalar"))

    return nc


_NC_CACHE = None


def kernel(L0_attn_x, L0_attn_Wk, L0_attn_Wq, L0_attn_Wv,
           L0_ffn_weight_q, L0_ffn_weight_scale, L0_ffn_bias):
    global _NC_CACHE
    x = np.ascontiguousarray(np.asarray(L0_attn_x, dtype=np.float32))
    wk = np.ascontiguousarray(np.asarray(L0_attn_Wk, dtype=np.float32))
    wq = np.ascontiguousarray(np.asarray(L0_attn_Wq, dtype=np.float32))
    wv = np.ascontiguousarray(np.asarray(L0_attn_Wv, dtype=np.float32))
    wf = np.ascontiguousarray(np.asarray(L0_ffn_weight_q, dtype=np.int8))
    wsc = np.ascontiguousarray(np.asarray(L0_ffn_weight_scale,
                                          dtype=np.float32))
    bias = np.ascontiguousarray(np.asarray(L0_ffn_bias, dtype=np.float32))

    wsc_bc = np.ascontiguousarray(np.broadcast_to(wsc[None, :], (128, D)))
    bias_bc = np.ascontiguousarray(np.broadcast_to(bias[None, :], (128, D)))
    ident = np.eye(128, dtype=ml_dtypes.bfloat16)
    mask = np.triu(np.full((128, 128), NEG, dtype=np.float32), k=1)

    if _NC_CACHE is None:
        _NC_CACHE = build_nc()
    nc = _NC_CACHE

    in_maps = []
    for b in range(8):
        in_maps.append({
            "x": np.ascontiguousarray(x[b]),
            "wk": wk, "wq": wq, "wv": wv, "wf": wf,
            "wsc": wsc_bc, "biasd": bias_bc,
            "ident": ident, "maskc": mask,
        })
    res = run_bass_kernel_spmd(nc, in_maps, core_ids=list(range(8)))
    return np.stack([np.asarray(r["y"]) for r in res.results], axis=0)
